# version 02
# baseline (speedup 1.0000x reference)
import contextlib
import numpy as np
import ml_dtypes
import concourse.bass as bass
import concourse.mybir as mybir
from concourse.bass_utils import run_bass_kernel_spmd

F32 = mybir.dt.float32
BF16 = mybir.dt.bfloat16
I32 = mybir.dt.int32
AF = mybir.ActivationFunctionType
ALU = mybir.AluOpType
AX = mybir.AxisListType
f32 = np.float32

NCORE = 8
D = 4096
T = 8192
TL = T // NCORE
NCH = D // 128
EXP_SHIFT = 16.0
INV_SQRT_HD = 1.0 / float(np.sqrt(128.0))
LN_EPS = 1e-5
CAPACITY = 3072.0
NEG = -1e30

ENGINES = ("sync", "gpsimd", "scalar", "vector", "tensor")


class Prog:

    def __init__(self, nc, es):
        self.nc = nc
        self.tok = es.enter_context(nc.semaphore("tok"))
        self.val = 0
        self.items = {e: [] for e in ENGINES}

    def op(self, eng, fn, inc=1):
        wait_at = self.val
        self.val += inc

        def emit(e, fn=fn, wait_at=wait_at, inc=inc, tok=self.tok):
            if wait_at > 0:
                e.wait_ge(tok, wait_at)
            fn(e).then_inc(tok, inc)
        self.items[eng].append(emit)

    def chain(self, eng, fns, inc=1):
        wait_at = self.val
        self.val += inc

        def emit(e, fns=fns, wait_at=wait_at, inc=inc, tok=self.tok):
            if wait_at > 0:
                e.wait_ge(tok, wait_at)
            ins = None
            for fn in fns:
                ins = fn(e)
            ins.then_inc(tok, inc)
        self.items[eng].append(emit)

    def finish(self, block):
        final = self.val

        for name in ENGINES:
            items = self.items[name]

            def body(e, items=items, final=final, tok=self.tok):
                for emit in items:
                    emit(e)
                e.wait_ge(tok, final)
            getattr(block, name)(body)


def dma(p, eng, out, in_):
    p.op(eng, lambda e, out=out, in_=in_: e.dma_start(out=out, in_=in_), inc=16)


def _ident():
    return np.eye(128, dtype=np.float32)


def _to_col(v):
    return np.ascontiguousarray(v.reshape(-1, 128).T)


def _tile_w(w_in_out):
    din, dout = w_in_out.shape
    a = w_in_out.reshape(din // 128, 128, dout // 128, 128)
    return np.ascontiguousarray(a.transpose(2, 1, 0, 3))



def build_A():
    nc = bass.Bass("TRN2", target_bir_lowering=False)
    x_in = nc.dram_tensor("x", [D, TL], F32, kind="ExternalInput").ap()
    gw1t_in = nc.dram_tensor("gw1t", [D, 32], F32, kind="ExternalInput").ap()
    gb1c_in = nc.dram_tensor("gb1c", [32, 1], F32, kind="ExternalInput").ap()
    gam_in = nc.dram_tensor("gam", [1, 32], F32, kind="ExternalInput").ap()
    bet_in = nc.dram_tensor("bet", [1, 32], F32, kind="ExternalInput").ap()
    gw2t_in = nc.dram_tensor("gw2t", [32, 8], F32, kind="ExternalInput").ap()
    gb2_in = nc.dram_tensor("gb2", [1, 8], F32, kind="ExternalInput").ap()
    id_in = nc.dram_tensor("ident", [128, 128], F32, kind="ExternalInput").ap()
    xn_out = nc.dram_tensor("xn", [D, TL], BF16, kind="ExternalOutput").ap()
    cnt_out = nc.dram_tensor("cnt", [1, 16], F32, kind="ExternalOutput").ap()
    wg_out = nc.dram_tensor("wg", [2, TL, 8], F32, kind="ExternalOutput").ap()

    es = contextlib.ExitStack()
    with es:
        E = es.enter_context
        xb = E(nc.sbuf_tensor("xb", [128, NCH, TL], BF16))
        xnb = E(nc.sbuf_tensor("xnb", [128, NCH, TL], BF16))
        sq = E(nc.sbuf_tensor("sq", [128, TL], BF16))
        xf = E(nc.sbuf_tensor("xf", [128, TL], F32))
        gw1b = E(nc.sbuf_tensor("gw1b", [128, NCH, 32], BF16))
        gb1c = E(nc.sbuf_tensor("gb1c_s", [32, 1], F32))
        gamr = E(nc.sbuf_tensor("gamr", [1, 32], F32))
        betr = E(nc.sbuf_tensor("betr", [1, 32], F32))
        gw2t = E(nc.sbuf_tensor("gw2ts", [32, 8], F32))
        gb2r = E(nc.sbuf_tensor("gb2r", [1, 8], F32))
        ident = E(nc.sbuf_tensor("identsb", [128, 128], F32))
        onesf = E(nc.sbuf_tensor("onesf", [1, 128], F32))
        onesb = E(nc.sbuf_tensor("onesb", [128, 1], BF16))
        mrow = E(nc.sbuf_tensor("mrow", [1, TL], F32))
        qrow = E(nc.sbuf_tensor("qrow", [1, TL], F32))
        vrow = E(nc.sbuf_tensor("vrow", [1, TL], F32))
        rrow = E(nc.sbuf_tensor("rrow", [1, TL], F32))
        mbc = E(nc.sbuf_tensor("mbc", [128, TL], F32))
        rbc = E(nc.sbuf_tensor("rbc", [128, TL], F32))
        hfm = E(nc.sbuf_tensor("hfm", [32, TL], F32))
        gamb = E(nc.sbuf_tensor("gamb", [128, 32], F32))
        betb = E(nc.sbuf_tensor("betb", [128, 32], F32))
        gb2b = E(nc.sbuf_tensor("gb2b", [128, 8], F32))
        htm = E(nc.sbuf_tensor("htm", [128, 32], F32))
        hm = E(nc.sbuf_tensor("hm", [128, 1], F32))
        hv = E(nc.sbuf_tensor("hv", [128, 1], F32))
        hrs = E(nc.sbuf_tensor("hrs", [128, 1], F32))
        hsq = E(nc.sbuf_tensor("hsq", [128, 32], F32))
        hft = E(nc.sbuf_tensor("hft", [32, 128], F32))
        gts = E(nc.sbuf_tensor("gts", [128, 8], F32))
        iotf = E(nc.sbuf_tensor("iotf", [128, 8], F32))
        ioti = E(nc.sbuf_tensor("ioti", [128, 8], I32))
        v1 = E(nc.sbuf_tensor("v1", [128, 1], F32))
        nv1 = E(nc.sbuf_tensor("nv1", [128, 1], F32))
        v2 = E(nc.sbuf_tensor("v2", [128, 1], F32))
        e2 = E(nc.sbuf_tensor("e2", [128, 1], F32))
        g1 = E(nc.sbuf_tensor("g1", [128, 1], F32))
        g2c = E(nc.sbuf_tensor("g2c", [128, 1], F32))
        sel8 = E(nc.sbuf_tensor("sel8", [128, 8], F32))
        m1c = E(nc.sbuf_tensor("m1c", [128, 1], F32))
        oh1 = E(nc.sbuf_tensor("oh1", [128, 8], F32))
        oh2 = E(nc.sbuf_tensor("oh2", [128, 8], F32))
        g2t = E(nc.sbuf_tensor("g2t", [128, 8], F32))
        wgall1 = E(nc.sbuf_tensor("wgall1", [128, 8, 8], F32))
        wgall2 = E(nc.sbuf_tensor("wgall2", [128, 8, 8], F32))
        cac1 = E(nc.sbuf_tensor("cac1", [128, 8], F32))
        cac2 = E(nc.sbuf_tensor("cac2", [128, 8], F32))
        cacb = E(nc.sbuf_tensor("cacb", [128, 8], BF16))
        cnt = E(nc.sbuf_tensor("cntsb", [1, 16], F32))
        psA = E(nc.psum_tensor("psA", [128, 4, 512], F32))
        psB = E(nc.psum_tensor("psB", [128, 4, 512], F32))
        block = E(nc.Block())
        p = Prog(nc, es)

        xr = x_in.rearrange("(c p) t -> p c t", p=128)
        gw1r = gw1t_in.rearrange("(c p) f -> p c f", p=128)
        xnr = xn_out.rearrange("(c p) t -> p c t", p=128)

        for i in range(4):
            dma(p, "gpsimd", xb[:, i * 8:(i + 1) * 8, :], xr[:, i * 8:(i + 1) * 8, :])
        dma(p, "gpsimd", gw1b[:], gw1r)
        p.op("gpsimd", lambda e: e.iota(ioti[:], pattern=[[1, 8]], base=0, channel_multiplier=0))
        dma(p, "sync", gb1c[:], gb1c_in)
        dma(p, "sync", gamr[:], gam_in)
        dma(p, "sync", betr[:], bet_in)
        dma(p, "sync", gw2t[:], gw2t_in)
        dma(p, "sync", gb2r[:], gb2_in)
        dma(p, "sync", ident[:], id_in)
        p.op("vector", lambda e: e.memset(onesf[:], 1.0))
        p.op("vector", lambda e: e.memset(onesb[:], 1.0))
        p.op("vector", lambda e: e.tensor_copy(iotf[:], ioti[:]))
        p.op("vector", lambda e: e.memset(cac1[:], 0.0))
        p.op("vector", lambda e: e.memset(cac2[:], 0.0))

        for c in range(NCH):
            p.op("scalar", lambda e, c=c: e.activation(sq[:], xb[:, c, :], AF.Square))
            st, sp = (c == 0), (c == NCH - 1)
            p.chain("tensor", [
                lambda e, c=c, st=st, sp=sp: e.matmul(psA[0:1, 0, :], onesb[:], xb[:, c, 0:512], start=st, stop=sp),
                lambda e, c=c, st=st, sp=sp: e.matmul(psA[0:1, 1, :], onesb[:], xb[:, c, 512:1024], start=st, stop=sp),
                lambda e, st=st, sp=sp: e.matmul(psA[0:1, 2, :], onesb[:], sq[:, 0:512], start=st, stop=sp),
                lambda e, st=st, sp=sp: e.matmul(psA[0:1, 3, :], onesb[:], sq[:, 512:1024], start=st, stop=sp),
            ])
        p.op("scalar", lambda e: e.activation(mrow[0:1, 0:512], psA[0:1, 0, :], AF.Copy, scale=f32(1.0 / D)))
        p.op("scalar", lambda e: e.activation(mrow[0:1, 512:1024], psA[0:1, 1, :], AF.Copy, scale=f32(1.0 / D)))
        p.op("scalar", lambda e: e.activation(qrow[0:1, 0:512], psA[0:1, 2, :], AF.Copy, scale=f32(1.0 / D)))
        p.op("scalar", lambda e: e.activation(qrow[0:1, 512:1024], psA[0:1, 3, :], AF.Copy, scale=f32(1.0 / D)))
        p.op("vector", lambda e: e.tensor_tensor(vrow[0:1, :], mrow[0:1, :], mrow[0:1, :], op=ALU.mult))
        p.op("vector", lambda e: e.tensor_tensor(vrow[0:1, :], qrow[0:1, :], vrow[0:1, :], op=ALU.subtract))
        p.op("vector", lambda e: e.tensor_scalar_add(vrow[0:1, :], vrow[0:1, :], LN_EPS))
        p.op("scalar", lambda e: e.activation(vrow[0:1, :], vrow[0:1, :], AF.Sqrt))
        p.op("vector", lambda e: e.reciprocal(rrow[0:1, :], vrow[0:1, :]))
        p.chain("tensor", [
            lambda e: e.matmul(psB[:, 0, :], onesf[:], mrow[0:1, 0:512], start=True, stop=True),
            lambda e: e.matmul(psB[:, 1, :], onesf[:], mrow[0:1, 512:1024], start=True, stop=True),
            lambda e: e.matmul(psB[:, 2, :], onesf[:], rrow[0:1, 0:512], start=True, stop=True),
            lambda e: e.matmul(psB[:, 3, :], onesf[:], rrow[0:1, 512:1024], start=True, stop=True),
        ])
        p.op("scalar", lambda e: e.copy(mbc[:, 0:512], psB[:, 0, :]))
        p.op("scalar", lambda e: e.copy(mbc[:, 512:1024], psB[:, 1, :]))
        p.op("scalar", lambda e: e.copy(rbc[:, 0:512], psB[:, 2, :]))
        p.op("scalar", lambda e: e.copy(rbc[:, 512:1024], psB[:, 3, :]))
        for c in range(NCH):
            p.op("vector", lambda e, c=c: e.tensor_copy(xf[:], xb[:, c, :]))
            p.op("vector", lambda e: e.tensor_tensor(xf[:], xf[:], mbc[:], op=ALU.subtract))
            p.op("vector", lambda e, c=c: e.tensor_tensor(xnb[:, c, :], xf[:], rbc[:], op=ALU.mult))
        for i in range(4):
            dma(p, "sync", xnr[:, i * 8:(i + 1) * 8, :], xnb[:, i * 8:(i + 1) * 8, :])

        for c in range(NCH):
            st, sp = (c == 0), (c == NCH - 1)
            p.chain("tensor", [
                lambda e, c=c, st=st, sp=sp: e.matmul(psA[0:32, 0, :], gw1b[:, c, :], xnb[:, c, 0:512], start=st, stop=sp),
                lambda e, c=c, st=st, sp=sp: e.matmul(psA[0:32, 1, :], gw1b[:, c, :], xnb[:, c, 512:1024], start=st, stop=sp),
            ])
        p.op("vector", lambda e: e.tensor_scalar(hfm[:, 0:512], psA[0:32, 0, :], gb1c[:], None, op0=ALU.add))
        p.op("vector", lambda e: e.tensor_scalar(hfm[:, 512:1024], psA[0:32, 1, :], gb1c[:], None, op0=ALU.add))
        p.chain("tensor", [
            lambda e: e.matmul(psB[:, 0, 0:32], onesf[:], gamr[:], start=True, stop=True),
            lambda e: e.matmul(psB[:, 1, 0:32], onesf[:], betr[:], start=True, stop=True),
            lambda e: e.matmul(psB[:, 2, 0:8], onesf[:], gb2r[:], start=True, stop=True),
        ])
        p.op("scalar", lambda e: e.copy(gamb[:], psB[:, 0, 0:32]))
        p.op("scalar", lambda e: e.copy(betb[:], psB[:, 1, 0:32]))
        p.op("scalar", lambda e: e.copy(gb2b[:], psB[:, 2, 0:8]))

        for t in range(8):
            p.op("tensor", lambda e, t=t: e.transpose(psB[:, 3, 0:32], hfm[:, t * 128:(t + 1) * 128], ident[0:32, 0:32]))
            p.op("scalar", lambda e: e.copy(htm[:], psB[:, 3, 0:32]))
            p.op("vector", lambda e: e.reduce_sum(hm[:], htm[:], axis=AX.X))
            p.op("vector", lambda e: e.tensor_scalar_mul(hm[:], hm[:], 1.0 / 32.0))
            p.op("vector", lambda e: e.tensor_scalar(htm[:], htm[:], hm[:], None, op0=ALU.subtract))
            p.op("vector", lambda e: e.tensor_tensor(hsq[:], htm[:], htm[:], op=ALU.mult))
            p.op("vector", lambda e: e.reduce_sum(hv[:], hsq[:], axis=AX.X))
            p.op("vector", lambda e: e.tensor_scalar_mul(hv[:], hv[:], 1.0 / 32.0))
            p.op("vector", lambda e: e.tensor_scalar_add(hv[:], hv[:], LN_EPS))
            p.op("scalar", lambda e: e.activation(hv[:], hv[:], AF.Sqrt))
            p.op("vector", lambda e: e.reciprocal(hrs[:], hv[:]))
            p.op("vector", lambda e: e.tensor_scalar(htm[:], htm[:], hrs[:], None, op0=ALU.mult))
            p.op("vector", lambda e: e.tensor_tensor(htm[:], htm[:], gamb[:], op=ALU.mult))
            p.op("vector", lambda e: e.tensor_tensor(htm[:], htm[:], betb[:], op=ALU.add))
            p.op("vector", lambda e: e.tensor_relu(htm[:], htm[:]))
            p.op("tensor", lambda e: e.transpose(psB[0:32, 3, 0:128], htm[:], ident[:]))
            p.op("scalar", lambda e: e.copy(hft[:], psB[0:32, 3, 0:128]))
            p.op("tensor", lambda e: e.matmul(psB[:, 2, 64:72], hft[:], gw2t[:], start=True, stop=True))
            p.op("vector", lambda e: e.tensor_tensor(gts[:], psB[:, 2, 64:72], gb2b[:], op=ALU.add))
            p.op("vector", lambda e: e.reduce_max(v1[:], gts[:], axis=AX.X))
            p.op("vector", lambda e: e.tensor_scalar(oh1[:], gts[:], v1[:], None, op0=ALU.is_equal))
            p.op("vector", lambda e: e.scalar_tensor_tensor(sel8[:], oh1[:], -256.0, iotf[:], op0=ALU.mult, op1=ALU.add))
            p.op("vector", lambda e: e.tensor_reduce(m1c[:], sel8[:], axis=AX.X, op=ALU.min))
            p.op("vector", lambda e: e.tensor_scalar(oh1[:], sel8[:], m1c[:], None, op0=ALU.is_equal))
            p.op("vector", lambda e: e.scalar_tensor_tensor(g2t[:], oh1[:], NEG, gts[:], op0=ALU.mult, op1=ALU.add))
            p.op("vector", lambda e: e.reduce_max(v2[:], g2t[:], axis=AX.X))
            p.op("vector", lambda e: e.tensor_scalar(oh2[:], g2t[:], v2[:], None, op0=ALU.is_equal))
            p.op("vector", lambda e: e.scalar_tensor_tensor(sel8[:], oh2[:], -256.0, iotf[:], op0=ALU.mult, op1=ALU.add))
            p.op("vector", lambda e: e.tensor_reduce(m1c[:], sel8[:], axis=AX.X, op=ALU.min))
            p.op("vector", lambda e: e.tensor_scalar(oh2[:], sel8[:], m1c[:], None, op0=ALU.is_equal))
            p.op("vector", lambda e: e.tensor_scalar_mul(nv1[:], v1[:], -1.0))
            p.op("scalar", lambda e: e.activation(e2[:], v2[:], AF.Exp, bias=nv1[:]))
            p.op("vector", lambda e: e.tensor_scalar_add(g1[:], e2[:], 1.0))
            p.op("vector", lambda e: e.reciprocal(g1[:], g1[:]))
            p.op("vector", lambda e: e.tensor_tensor(g2c[:], e2[:], g1[:], op=ALU.mult))
            p.op("vector", lambda e, t=t: e.tensor_scalar(wgall1[:, t, :], oh1[:], g1[:], None, op0=ALU.mult))
            p.op("vector", lambda e, t=t: e.tensor_scalar(wgall2[:, t, :], oh2[:], g2c[:], None, op0=ALU.mult))
            p.op("vector", lambda e: e.tensor_tensor(cac1[:], cac1[:], oh1[:], op=ALU.add))
            p.op("vector", lambda e: e.tensor_tensor(cac2[:], cac2[:], oh2[:], op=ALU.add))

        p.op("vector", lambda e: e.tensor_copy(cacb[:], cac1[:]))
        p.op("tensor", lambda e: e.matmul(psB[0:1, 0, 128:136], onesb[:], cacb[:], start=True, stop=True))
        p.op("scalar", lambda e: e.copy(cnt[0:1, 0:8], psB[0:1, 0, 128:136]))
        p.op("vector", lambda e: e.tensor_copy(cacb[:], cac2[:]))
        p.op("tensor", lambda e: e.matmul(psB[0:1, 0, 136:144], onesb[:], cacb[:], start=True, stop=True))
        p.op("scalar", lambda e: e.copy(cnt[0:1, 8:16], psB[0:1, 0, 136:144]))
        dma(p, "sync", cnt_out, cnt[:])
        dma(p, "sync", wg_out[0].rearrange("(t p) e -> p t e", p=128), wgall1[:])
        dma(p, "sync", wg_out[1].rearrange("(t p) e -> p t e", p=128), wgall2[:])
        p.finish(block)
    return nc



def build_attn(l1: bool):
    nc = bass.Bass("TRN2", target_bir_lowering=False)
    xn4_in = nc.dram_tensor("xn4", [32, 128, NCH, 256], BF16, kind="ExternalInput").ap()
    wq_in = nc.dram_tensor("wq4", [128, NCH, 512], F32, kind="ExternalInput").ap()
    if l1:
        wk_in = nc.dram_tensor("wk4", [128, NCH, 512], F32, kind="ExternalInput").ap()
        wv_in = nc.dram_tensor("wv4", [128, NCH, 512], F32, kind="ExternalInput").ap()
        bq_in = nc.dram_tensor("bqc", [128, 4], F32, kind="ExternalInput").ap()
        bk_in = nc.dram_tensor("bkc", [128, 4], F32, kind="ExternalInput").ap()
        bv_in = nc.dram_tensor("bvr", [1, 512], F32, kind="ExternalInput").ap()
    else:
        wkv_in = nc.dram_tensor("wkv4", [128, NCH, 256], F32, kind="ExternalInput").ap()
    tri_in = nc.dram_tensor("tri", [128, 128], F32, kind="ExternalInput").ap()
    ctx_out = nc.dram_tensor("ctx", [512, T], BF16, kind="ExternalOutput").ap()

    q_scr = nc.dram_tensor("q_scr", [512, T], BF16)

    es = contextlib.ExitStack()
    with es:
        E = es.enter_context
        xt = E(nc.sbuf_tensor("xt", [128, NCH, 256], BF16))
        wsh = E(nc.sbuf_tensor("wsh", [128, NCH, 512], BF16))
        if l1:
            bqc = E(nc.sbuf_tensor("bqc_s", [128, 4], F32))
            bkc = E(nc.sbuf_tensor("bkc_s", [128, 4], F32))
            bvr = E(nc.sbuf_tensor("bvr_s", [1, 512], F32))
            bvb = E(nc.sbuf_tensor("bvb", [128, 512], F32))
            kres = E(nc.sbuf_tensor("kres", [128, 4, T], BF16))
            vres = E(nc.sbuf_tensor("vres", [128, 64, 512], BF16))
        else:
            wkvb = E(nc.sbuf_tensor("wkvb", [128, NCH, 256], BF16))
            kres = E(nc.sbuf_tensor("kres", [128, 1, T], BF16))
            vres = E(nc.sbuf_tensor("vres", [128, 64, 128], BF16))
        tri = E(nc.sbuf_tensor("tri_s", [128, 128], F32))
        qt = E(nc.sbuf_tensor("qt", [128, 512], BF16))
        qsb = E(nc.sbuf_tensor("qsb", [128, 256], BF16))
        ptb = E(nc.sbuf_tensor("ptb", [128, 512], BF16))
        rs = E(nc.sbuf_tensor("rs", [1, 512], F32))
        rcb = E(nc.sbuf_tensor("rcb", [128, 512], F32))
        cts = E(nc.sbuf_tensor("cts", [128, 512], BF16))
        onesf = E(nc.sbuf_tensor("onesf", [1, 128], F32))
        onesb = E(nc.sbuf_tensor("onesb", [128, 1], BF16))
        psA = E(nc.psum_tensor("psA", [128, 4, 512], F32))
        psB = E(nc.psum_tensor("psB", [128, 4, 512], F32))
        block = E(nc.Block())
        p = Prog(nc, es)

        p.op("vector", lambda e: e.memset(onesf[:], 1.0))
        p.op("vector", lambda e: e.memset(onesb[:], 1.0))
        dma(p, "sync", tri[:], tri_in)
        if l1:
            dma(p, "sync", bqc[:], bq_in)
            dma(p, "sync", bkc[:], bk_in)
            dma(p, "sync", bvr[:], bv_in)
            p.op("tensor", lambda e: e.matmul(psB[:, 0, :], onesf[:], bvr[:], start=True, stop=True))
            p.op("scalar", lambda e: e.copy(bvb[:], psB[:, 0, :]))

        def load_w(src_ap):
            for i in range(4):
                sl = slice(i * 8, (i + 1) * 8)
                dma(p, "gpsimd", wsh[:, sl, :], src_ap[:, sl, :])

        if not l1:
            load_w(wq_in)
            for i in range(4):
                sl = slice(i * 8, (i + 1) * 8)
                dma(p, "gpsimd", wkvb[:, sl, :], wkv_in[:, sl, :])
            for tb in range(32):
                dma(p, "sync", xt[:], xn4_in[tb])
                for oc in range(4):
                    p.chain("tensor", [
                        lambda e, c=c, oc=oc, st=(c == 0), sp=(c == NCH - 1): e.matmul(
                            psA[:, 0, 0:256], wsh[:, c, oc * 128:(oc + 1) * 128], xt[:, c, :], start=st, stop=sp)
                        for c in range(NCH)])
                    p.op("scalar", lambda e: e.copy(qsb[:], psA[:, 0, 0:256]))
                    dma(p, "sync", q_scr.ap()[oc * 128:(oc + 1) * 128, tb * 256:(tb + 1) * 256], qsb[:])
                p.chain("tensor", [
                    lambda e, c=c, st=(c == 0), sp=(c == NCH - 1): e.matmul(
                        psA[:, 1, 0:256], wkvb[:, c, 0:128], xt[:, c, :], start=st, stop=sp)
                    for c in range(NCH)])
                p.op("scalar", lambda e, tb=tb: e.copy(kres[:, 0, tb * 256:(tb + 1) * 256], psA[:, 1, 0:256]))
                for vt in range(2):
                    ci = tb * 2 + vt
                    p.chain("tensor", [
                        lambda e, c=c, vt=vt, st=(c == 0), sp=(c == NCH - 1): e.matmul(
                            psA[:, 2, 0:128], xt[:, c, vt * 128:(vt + 1) * 128], wkvb[:, c, 128:256], start=st, stop=sp)
                        for c in range(NCH)])
                    p.op("scalar", lambda e, ci=ci: e.copy(vres[:, ci, :], psA[:, 2, 0:128]))
        else:
            load_w(wk_in)
            for tb in range(32):
                dma(p, "sync", xt[:], xn4_in[tb])
                for kc in range(4):
                    p.chain("tensor", [
                        lambda e, c=c, kc=kc, st=(c == 0), sp=(c == NCH - 1): e.matmul(
                            psA[:, 0, 0:256], wsh[:, c, kc * 128:(kc + 1) * 128], xt[:, c, :], start=st, stop=sp)
                        for c in range(NCH)])
                    p.op("vector", lambda e, kc=kc, tb=tb: e.tensor_scalar(
                        kres[:, kc, tb * 256:(tb + 1) * 256], psA[:, 0, 0:256], bkc[:, kc:kc + 1], None, op0=ALU.add))
            load_w(wv_in)
            for tb in range(32):
                dma(p, "sync", xt[:], xn4_in[tb])
                for vt in range(2):
                    ci = tb * 2 + vt
                    p.chain("tensor", [
                        lambda e, c=c, vt=vt, st=(c == 0), sp=(c == NCH - 1): e.matmul(
                            psA[:, 2, :], xt[:, c, vt * 128:(vt + 1) * 128], wsh[:, c, :], start=st, stop=sp)
                        for c in range(NCH)])
                    p.op("vector", lambda e, ci=ci: e.tensor_tensor(vres[:, ci, :], psA[:, 2, :], bvb[:], op=ALU.add))
            load_w(wq_in)
            for tb in range(32):
                dma(p, "sync", xt[:], xn4_in[tb])
                for oc in range(4):
                    p.chain("tensor", [
                        lambda e, c=c, oc=oc, st=(c == 0), sp=(c == NCH - 1): e.matmul(
                            psA[:, 0, 0:256], wsh[:, c, oc * 128:(oc + 1) * 128], xt[:, c, :], start=st, stop=sp)
                        for c in range(NCH)])
                    p.op("vector", lambda e, oc=oc: e.tensor_scalar(
                        qsb[:], psA[:, 0, 0:256], bqc[:, oc:oc + 1], None, op0=ALU.add))
                    dma(p, "sync", q_scr.ap()[oc * 128:(oc + 1) * 128, tb * 256:(tb + 1) * 256], qsb[:])

        for lh in range(4):
            for qg in range(16):
                dma(p, "sync", qt[:], q_scr.ap()[lh * 128:(lh + 1) * 128, qg * 512:(qg + 1) * 512])
                nch_k = 64 if l1 else 4 * (qg + 1)
                for c in range(nch_k):
                    d = c - 4 * qg
                    w0 = 128 * d if (not l1 and d >= 0) else 0
                    diag = (not l1) and d >= 0
                    klh = lh if l1 else 0
                    p.op("tensor", lambda e, c=c, klh=klh, w0=w0: e.matmul(
                        psA[:, 0, w0:512], kres[:, klh, c * 128:(c + 1) * 128], qt[:, w0:512],
                        start=True, stop=True))
                    if diag:
                        p.op("vector", lambda e, w0=w0: e.tensor_tensor(
                            psA[:, 0, w0:w0 + 128], psA[:, 0, w0:w0 + 128], tri[:], op=ALU.add))
                    p.op("scalar", lambda e, w0=w0: e.activation(
                        ptb[:, w0:512], psA[:, 0, w0:512], AF.Exp,
                        bias=f32(-EXP_SHIFT), scale=f32(INV_SQRT_HD)))
                    st = (c == 0)
                    sp = (c == nch_k - 1)
                    voff = lh * 128 if l1 else 0
                    p.chain("tensor", [
                        lambda e, c=c, w0=w0, st=st, sp=sp, voff=voff: e.matmul(
                            psB[:, 0, w0:512], vres[:, c, voff:voff + 128], ptb[:, w0:512], start=st, stop=sp),
                        lambda e, c=c, w0=w0, st=st, sp=sp: e.matmul(
                            psB[0:1, 2, w0:512], onesb[:], ptb[:, w0:512], start=st, stop=sp),
                    ])
                p.op("vector", lambda e: e.reciprocal(rs[:], psB[0:1, 2, :]))
                p.op("tensor", lambda e: e.matmul(psB[:, 1, :], onesf[:], rs[:], start=True, stop=True))
                p.op("scalar", lambda e: e.copy(rcb[:], psB[:, 1, :]))
                p.op("vector", lambda e: e.tensor_tensor(cts[:], psB[:, 0, :], rcb[:], op=ALU.mult))
                dma(p, "sync", ctx_out[lh * 128:(lh + 1) * 128, qg * 512:(qg + 1) * 512], cts[:])
        p.finish(block)
    return nc



def build_D():
    nc = bass.Bass("TRN2", target_bir_lowering=False)
    xn1_in = nc.dram_tensor("xn1", [D, TL], BF16, kind="ExternalInput").ap()
    wq4_in = nc.dram_tensor("wq4", [32, 128, NCH, 128], F32, kind="ExternalInput").ap()
    wk4_in = nc.dram_tensor("wk4", [32, 128, NCH, 128], F32, kind="ExternalInput").ap()
    wv4_in = nc.dram_tensor("wv4", [32, 128, NCH, 128], F32, kind="ExternalInput").ap()
    bq_in = nc.dram_tensor("bqc", [128, 32], F32, kind="ExternalInput").ap()
    bk_in = nc.dram_tensor("bkc", [128, 32], F32, kind="ExternalInput").ap()
    bv_in = nc.dram_tensor("bvc", [128, 32], F32, kind="ExternalInput").ap()
    idb_in = nc.dram_tensor("identb", [128, 128], BF16, kind="ExternalInput").ap()
    ctx_out = nc.dram_tensor("ctx", [D, TL], BF16, kind="ExternalOutput").ap()

    es = contextlib.ExitStack()
    with es:
        E = es.enter_context
        xnb = E(nc.sbuf_tensor("xnb", [128, NCH, TL], BF16))
        qstr = E(nc.sbuf_tensor("qstr", [128, 32, 512], BF16))
        kstr = E(nc.sbuf_tensor("kstr", [128, 32, 512], BF16))
        vstr = E(nc.sbuf_tensor("vstr", [128, 32, 512], BF16))
        ctb2 = E(nc.sbuf_tensor("ctb2", [128, 32, 256], BF16))
        wtl = E(nc.sbuf_tensor("wtl", [128, NCH, 128], BF16))
        bqc = E(nc.sbuf_tensor("bqc_s", [128, 32], F32))
        bkc = E(nc.sbuf_tensor("bkc_s", [128, 32], F32))
        bvc = E(nc.sbuf_tensor("bvc_s", [128, 32], F32))
        identb = E(nc.sbuf_tensor("identb_s", [128, 128], BF16))
        pt = E(nc.sbuf_tensor("pt", [32, 512], BF16))
        rs = E(nc.sbuf_tensor("rs", [1, 512], F32))
        rb = E(nc.sbuf_tensor("rb", [128, 512], F32))
        vts = E(nc.sbuf_tensor("vts", [32, 16, 128], BF16))
        onesf = E(nc.sbuf_tensor("onesf", [1, 128], F32))
        ones32 = E(nc.sbuf_tensor("ones32", [32, 1], BF16))
        psA = E(nc.psum_tensor("psA", [128, 4, 512], F32))
        psVt = E(nc.psum_tensor("psVt", [32, 2, 1024], BF16))
        block = E(nc.Block())
        p = Prog(nc, es)

        xn1r = xn1_in.rearrange("(c p) t -> p c t", p=128)
        ctxr = ctx_out.rearrange("(h p) n -> p h n", p=128)

        p.op("vector", lambda e: e.memset(onesf[:], 1.0))
        p.op("vector", lambda e: e.memset(ones32[:], 1.0))
        dma(p, "sync", bqc[:], bq_in)
        dma(p, "sync", bkc[:], bk_in)
        dma(p, "sync", bvc[:], bv_in)
        dma(p, "sync", identb[:], idb_in)
        for i in range(4):
            sl = slice(i * 8, (i + 1) * 8)
            dma(p, "sync", xnb[:, sl, :], xn1r[:, sl, :])

        for half in range(2):
            hs = slice(half * 512, (half + 1) * 512)
            for name, w4, bc, dst in (("q", wq4_in, bqc, qstr), ("k", wk4_in, bkc, kstr), ("v", wv4_in, bvc, vstr)):
                for oc in range(32):
                    dma(p, "gpsimd", wtl[:], w4[oc])
                    p.chain("tensor", [
                        lambda e, c=c, hs=hs, st=(c == 0), sp=(c == NCH - 1): e.matmul(
                            psA[:, 0, :], wtl[:, c, :], xnb[:, c, hs], start=st, stop=sp)
                        for c in range(NCH)])
                    p.op("vector", lambda e, oc=oc, bc=bc, dst=dst: e.tensor_scalar(
                        dst[:, oc, :], psA[:, 0, :], bc[:, oc:oc + 1], None, op0=ALU.add))
            for sub in range(2):
                for g in range(16):
                    hh = sub * 256 + g * 16
                    ng = g * 16
                    p.chain("tensor", [
                        lambda e, j=j, hh=hh: e.matmul(
                            psA[0:32, 1, j * 32:(j + 1) * 32], kstr[:, :, hh + j],
                            qstr[:, :, hh + j], start=True, stop=True)
                        for j in range(16)])
                    p.op("scalar", lambda e: e.activation(
                        pt[:], psA[0:32, 1, :], AF.Exp, bias=f32(-EXP_SHIFT), scale=f32(INV_SQRT_HD)))
                    p.op("tensor", lambda e: e.matmul(psA[0:1, 2, :], ones32[:], pt[:], start=True, stop=True))
                    p.op("vector", lambda e: e.reciprocal(rs[:], psA[0:1, 2, :]))
                    p.op("tensor", lambda e: e.matmul(psA[:, 2, :], onesf[:], rs[:], start=True, stop=True))
                    p.op("scalar", lambda e: e.copy(rb[:], psA[:, 2, :]))
                    p.chain("tensor", [
                        lambda e, j=j, hh=hh: e.transpose(
                            psVt[0:32, j // 8, (j % 8) * 128:(j % 8 + 1) * 128],
                            vstr[:, :, hh + j], identb[:])
                        for j in range(16)])
                    p.op("scalar", lambda e: e.copy(vts[:, 0:8, :], psVt[0:32, 0, :].rearrange("p (j d) -> p j d", d=128)))
                    p.op("scalar", lambda e: e.copy(vts[:, 8:16, :], psVt[0:32, 1, :].rearrange("p (j d) -> p j d", d=128)))
                    p.chain("tensor", [
                        lambda e, j=j: e.matmul(
                            psA[:, 3, j * 32:(j + 1) * 32], vts[:, j, :], pt[:, j * 32:(j + 1) * 32],
                            start=True, stop=True)
                        for j in range(16)])
                    p.op("vector", lambda e, ng=ng: e.tensor_tensor(
                        ctb2[:, :, ng:ng + 16].rearrange("p h n -> p n h"),
                        psA[:, 3, :].rearrange("p (n h) -> p n h", h=32), rb[:].rearrange("p (n h) -> p n h", h=32),
                        op=ALU.mult))
                for i in range(4):
                    sl = slice(i * 8, (i + 1) * 8)
                    dma(p, "sync", ctxr[:, sl, half * 512 + sub * 256:half * 512 + (sub + 1) * 256], ctb2[:, sl, :])
        p.finish(block)
    return nc



def build_C():
    nc = bass.Bass("TRN2", target_bir_lowering=False)
    x_in = nc.dram_tensor("x", [D, TL], F32, kind="ExternalInput").ap()
    xn_in = nc.dram_tensor("xn", [D, TL], BF16, kind="ExternalInput").ap()
    ctx_in = nc.dram_tensor("ctxt", [D, TL], BF16, kind="ExternalInput").ap()
    wo4_in = nc.dram_tensor("wo4", [32, 128, NCH, 128], F32, kind="ExternalInput").ap()
    fw4_in = nc.dram_tensor("fw4", [32, 128, NCH, 128], F32, kind="ExternalInput").ap()
    boc_in = nc.dram_tensor("boc", [128, 32], F32, kind="ExternalInput").ap()
    la4_in = nc.dram_tensor("la4", [128, NCH, 128], F32, kind="ExternalInput").ap()
    lb_in = nc.dram_tensor("lb", [128, D], F32, kind="ExternalInput").ap()
    cnts_in = nc.dram_tensor("cnts", [16, 8], F32, kind="ExternalInput").ap()
    sel2_in = nc.dram_tensor("sel2", [16, 2], F32, kind="ExternalInput").ap()
    wg_in = nc.dram_tensor("wg", [2, TL, 8], F32, kind="ExternalInput").ap()
    ssel_in = nc.dram_tensor("ssel", [8, 128], F32, kind="ExternalInput").ap()
    id_in = nc.dram_tensor("ident", [128, 128], F32, kind="ExternalInput").ap()
    x1_out = nc.dram_tensor("x1", [D, TL], F32, kind="ExternalOutput").ap()
    xn1_out = nc.dram_tensor("xn1", [D, TL], BF16, kind="ExternalOutput").ap()

    es = contextlib.ExitStack()
    with es:
        E = es.enter_context
        xnb = E(nc.sbuf_tensor("xnb", [128, NCH, TL], BF16))
        ctb = E(nc.sbuf_tensor("ctb", [128, NCH, TL], BF16))
        lab = E(nc.sbuf_tensor("lab", [128, NCH, 128], BF16))
        lbb = E(nc.sbuf_tensor("lbb", [128, D], BF16))
        wob = E(nc.sbuf_tensor("wob", [128, NCH, 128], BF16))
        fwb = E(nc.sbuf_tensor("fwb", [128, NCH, 128], BF16))
        boc = E(nc.sbuf_tensor("boc_s", [128, 32], F32))
        cnt16 = E(nc.sbuf_tensor("cnt16", [16, 8], F32))
        sel2 = E(nc.sbuf_tensor("sel2_s", [16, 2], F32))
        ssel = E(nc.sbuf_tensor("ssel_s", [8, 128], F32))
        ident = E(nc.sbuf_tensor("identsb", [128, 128], F32))
        cgrow = E(nc.sbuf_tensor("cgrow", [1, 16], F32))
        alw = E(nc.sbuf_tensor("alw", [1, 16], F32))
        alb1 = E(nc.sbuf_tensor("alb1", [128, 8], F32))
        alb2 = E(nc.sbuf_tensor("alb2", [128, 8], F32))
        wg1t = E(nc.sbuf_tensor("wg1t", [128, 8], F32))
        wg2t = E(nc.sbuf_tensor("wg2t", [128, 8], F32))
        wt = E(nc.sbuf_tensor("wt", [128, 8], F32))
        wfm = E(nc.sbuf_tensor("wfm", [8, TL], F32))
        wbc = E(nc.sbuf_tensor("wbc", [128, TL], F32))
        us = E(nc.sbuf_tensor("us", [128, TL], BF16))
        ssb = E(nc.sbuf_tensor("ssb", [128, TL], F32))
        gsb = E(nc.sbuf_tensor("gsb", [128, TL], F32))
        x1bb = E(nc.sbuf_tensor("x1bb", [128, TL], BF16))
        sqx = E(nc.sbuf_tensor("sqx", [128, TL], BF16))
        onesf = E(nc.sbuf_tensor("onesf", [1, 128], F32))
        onesb = E(nc.sbuf_tensor("onesb", [128, 1], BF16))
        mrow = E(nc.sbuf_tensor("mrow", [1, TL], F32))
        qrow = E(nc.sbuf_tensor("qrow", [1, TL], F32))
        vrow = E(nc.sbuf_tensor("vrow", [1, TL], F32))
        rrow = E(nc.sbuf_tensor("rrow", [1, TL], F32))
        psA = E(nc.psum_tensor("psA", [128, 4, 512], F32))
        psB = E(nc.psum_tensor("psB", [128, 4, 512], F32))
        block = E(nc.Block())
        p = Prog(nc, es)
        mbc = wbc
        rbc = gsb

        xr = x_in.rearrange("(c p) t -> p c t", p=128)
        xnr = xn_in.rearrange("(c p) t -> p c t", p=128)
        ctr = ctx_in.rearrange("(c p) t -> p c t", p=128)
        lar = la4_in
        x1r = x1_out.rearrange("(c p) t -> p c t", p=128)
        xn1r = xn1_out.rearrange("(c p) t -> p c t", p=128)

        p.op("vector", lambda e: e.memset(onesf[:], 1.0))
        p.op("vector", lambda e: e.memset(onesb[:], 1.0))
        for i in range(4):
            sl = slice(i * 8, (i + 1) * 8)
            dma(p, "sync", xnb[:, sl, :], xnr[:, sl, :])
            dma(p, "sync", ctb[:, sl, :], ctr[:, sl, :])
        dma(p, "gpsimd", lab[:], lar)
        dma(p, "gpsimd", lbb[:], lb_in)
        dma(p, "sync", boc[:], boc_in)
        dma(p, "sync", cnt16[:], cnts_in)
        dma(p, "sync", sel2[:], sel2_in)
        dma(p, "sync", ssel[:], ssel_in)
        dma(p, "sync", ident[:], id_in)

        p.op("tensor", lambda e: e.matmul(psA[0:1, 0, 0:8], sel2[:, 0:1], cnt16[:], start=True, stop=True))
        p.op("tensor", lambda e: e.matmul(psA[0:1, 1, 0:8], sel2[:, 1:2], cnt16[:], start=True, stop=True))
        p.op("scalar", lambda e: e.copy(cgrow[0:1, 0:8], psA[0:1, 0, 0:8]))
        p.op("scalar", lambda e: e.copy(cgrow[0:1, 8:16], psA[0:1, 1, 0:8]))
        p.op("vector", lambda e: e.tensor_scalar(alw[:], cgrow[:], CAPACITY, None, op0=ALU.is_le))
        p.op("tensor", lambda e: e.matmul(psA[:, 2, 0:8], onesf[:], alw[0:1, 0:8], start=True, stop=True))
        p.op("tensor", lambda e: e.matmul(psA[:, 3, 0:8], onesf[:], alw[0:1, 8:16], start=True, stop=True))
        p.op("scalar", lambda e: e.copy(alb1[:], psA[:, 2, 0:8]))
        p.op("scalar", lambda e: e.copy(alb2[:], psA[:, 3, 0:8]))

        for t in range(8):
            dma(p, "sync", wg1t[:], wg_in[0, t * 128:(t + 1) * 128, :])
            dma(p, "sync", wg2t[:], wg_in[1, t * 128:(t + 1) * 128, :])
            p.op("vector", lambda e: e.tensor_tensor(wt[:], wg1t[:], alb1[:], op=ALU.mult))
            p.op("vector", lambda e: e.tensor_tensor(wg2t[:], wg2t[:], alb2[:], op=ALU.mult))
            p.op("vector", lambda e: e.tensor_tensor(wt[:], wt[:], wg2t[:], op=ALU.add))
            p.op("tensor", lambda e: e.transpose(psB[0:8, 0, 0:128], wt[:], ident[:]))
            p.op("scalar", lambda e, t=t: e.copy(wfm[:, t * 128:(t + 1) * 128], psB[0:8, 0, 0:128]))
        p.op("tensor", lambda e: e.matmul(psB[:, 1, :], ssel[:], wfm[:, 0:512], start=True, stop=True))
        p.op("tensor", lambda e: e.matmul(psB[:, 2, :], ssel[:], wfm[:, 512:1024], start=True, stop=True))
        p.op("scalar", lambda e: e.activation(wbc[:, 0:512], psB[:, 1, :], AF.Copy, scale=f32(1.0 / 16.0)))
        p.op("scalar", lambda e: e.activation(wbc[:, 512:1024], psB[:, 2, :], AF.Copy, scale=f32(1.0 / 16.0)))

        for half in range(2):
            mms = [lambda e, c=c, half=half, st=(c == 0), sp=(c == NCH - 1):
                   e.matmul(psA[:, half, :], lab[:, c, :], xnb[:, c, half * 512:(half + 1) * 512], start=st, stop=sp)
                   for c in range(NCH)]
            p.chain("tensor", mms)
        p.op("vector", lambda e: e.tensor_tensor(us[:, 0:512], psA[:, 0, :], wbc[:, 0:512], op=ALU.mult))
        p.op("vector", lambda e: e.tensor_tensor(us[:, 512:1024], psA[:, 1, :], wbc[:, 512:1024], op=ALU.mult))

        for oc in range(32):
            dma(p, "gpsimd", wob[:], wo4_in[oc])
            dma(p, "gpsimd", fwb[:], fw4_in[oc])
            for half in range(2):
                hs = slice(half * 512, (half + 1) * 512)
                mms = [lambda e, c=c, half=half, hs=hs, st=(c == 0), sp=(c == NCH - 1):
                       e.matmul(psA[:, half, :], wob[:, c, :], ctb[:, c, hs], start=st, stop=sp)
                       for c in range(NCH)]
                p.chain("tensor", mms)
                mms2 = [lambda e, c=c, half=half, hs=hs:
                        e.matmul(psA[:, 2 + half, :], fwb[:, c, :], xnb[:, c, hs], start=(c == 0), stop=False)
                        for c in range(NCH)]
                mms2.append(lambda e, half=half, hs=hs, oc=oc: e.matmul(
                    psA[:, 2 + half, :], lbb[:, oc * 128:(oc + 1) * 128], us[:, hs], start=False, stop=True))
                p.chain("tensor", mms2)
            p.op("vector", lambda e, oc=oc: e.tensor_scalar(ssb[:, 0:512], psA[:, 0, :], boc[:, oc:oc + 1], None, op0=ALU.add))
            p.op("vector", lambda e, oc=oc: e.tensor_scalar(ssb[:, 512:1024], psA[:, 1, :], boc[:, oc:oc + 1], None, op0=ALU.add))
            p.op("scalar", lambda e: e.activation(gsb[:, 0:512], psA[:, 2, :], AF.Silu))
            p.op("scalar", lambda e: e.activation(gsb[:, 512:1024], psA[:, 3, :], AF.Silu))
            p.op("vector", lambda e: e.tensor_tensor(ssb[:], ssb[:], gsb[:], op=ALU.mult))
            dma(p, "sync", gsb[:], xr[:, oc, :])
            p.op("vector", lambda e: e.tensor_tensor(ssb[:], gsb[:], ssb[:], op=ALU.add))
            dma(p, "sync", x1r[:, oc, :], ssb[:])
            p.op("vector", lambda e: e.tensor_copy(x1bb[:], ssb[:]))
            p.op("scalar", lambda e: e.activation(sqx[:], x1bb[:], AF.Square))
            st, sp = (oc == 0), (oc == 31)
            p.chain("tensor", [
                lambda e, st=st, sp=sp: e.matmul(psB[0:1, 0, :], onesb[:], x1bb[:, 0:512], start=st, stop=sp),
                lambda e, st=st, sp=sp: e.matmul(psB[0:1, 1, :], onesb[:], x1bb[:, 512:1024], start=st, stop=sp),
                lambda e, st=st, sp=sp: e.matmul(psB[0:1, 2, :], onesb[:], sqx[:, 0:512], start=st, stop=sp),
                lambda e, st=st, sp=sp: e.matmul(psB[0:1, 3, :], onesb[:], sqx[:, 512:1024], start=st, stop=sp),
            ])

        p.op("scalar", lambda e: e.activation(mrow[0:1, 0:512], psB[0:1, 0, :], AF.Copy, scale=f32(1.0 / D)))
        p.op("scalar", lambda e: e.activation(mrow[0:1, 512:1024], psB[0:1, 1, :], AF.Copy, scale=f32(1.0 / D)))
        p.op("scalar", lambda e: e.activation(qrow[0:1, 0:512], psB[0:1, 2, :], AF.Copy, scale=f32(1.0 / D)))
        p.op("scalar", lambda e: e.activation(qrow[0:1, 512:1024], psB[0:1, 3, :], AF.Copy, scale=f32(1.0 / D)))
        p.op("vector", lambda e: e.tensor_tensor(vrow[0:1, :], mrow[0:1, :], mrow[0:1, :], op=ALU.mult))
        p.op("vector", lambda e: e.tensor_tensor(vrow[0:1, :], qrow[0:1, :], vrow[0:1, :], op=ALU.subtract))
        p.op("vector", lambda e: e.tensor_scalar_add(vrow[0:1, :], vrow[0:1, :], LN_EPS))
        p.op("scalar", lambda e: e.activation(vrow[0:1, :], vrow[0:1, :], AF.Sqrt))
        p.op("vector", lambda e: e.reciprocal(rrow[0:1, :], vrow[0:1, :]))
        p.chain("tensor", [
            lambda e: e.matmul(psB[:, 0, :], onesf[:], mrow[0:1, 0:512], start=True, stop=True),
            lambda e: e.matmul(psB[:, 1, :], onesf[:], mrow[0:1, 512:1024], start=True, stop=True),
            lambda e: e.matmul(psB[:, 2, :], onesf[:], rrow[0:1, 0:512], start=True, stop=True),
            lambda e: e.matmul(psB[:, 3, :], onesf[:], rrow[0:1, 512:1024], start=True, stop=True),
        ])
        p.op("scalar", lambda e: e.copy(mbc[:, 0:512], psB[:, 0, :]))
        p.op("scalar", lambda e: e.copy(mbc[:, 512:1024], psB[:, 1, :]))
        p.op("scalar", lambda e: e.copy(rbc[:, 0:512], psB[:, 2, :]))
        p.op("scalar", lambda e: e.copy(rbc[:, 512:1024], psB[:, 3, :]))
        for oc in range(32):
            dma(p, "gpsimd", x1bb[:], x1r[:, oc, :])
            p.op("vector", lambda e: e.tensor_copy(ssb[:], x1bb[:]))
            p.op("vector", lambda e: e.tensor_tensor(ssb[:], ssb[:], mbc[:], op=ALU.subtract))
            p.op("vector", lambda e: e.tensor_tensor(x1bb[:], ssb[:], rbc[:], op=ALU.mult))
            dma(p, "sync", xn1r[:, oc, :], x1bb[:])
        p.finish(block)
    return nc



def build_E():
    nc = bass.Bass("TRN2", target_bir_lowering=False)
    x1_in = nc.dram_tensor("x1", [D, TL], F32, kind="ExternalInput").ap()
    xn1_in = nc.dram_tensor("xn1", [D, TL], BF16, kind="ExternalInput").ap()
    ctx_in = nc.dram_tensor("ctxt", [D, TL], BF16, kind="ExternalInput").ap()
    w1t4_in = nc.dram_tensor("w1t4", [32, 128, NCH, 128], F32, kind="ExternalInput").ap()
    w1n4_in = nc.dram_tensor("w1n4", [32, 128, NCH, 128], F32, kind="ExternalInput").ap()
    a1t4_in = nc.dram_tensor("a1t4", [128, NCH, 32], F32, kind="ExternalInput").ap()
    a1_in = nc.dram_tensor("a1", [32, D], F32, kind="ExternalInput").ap()
    b1t_in = nc.dram_tensor("b1t", [32, D], F32, kind="ExternalInput").ap()
    mag_in = nc.dram_tensor("magc", [128, 32], F32, kind="ExternalInput").ap()
    x2_out = nc.dram_tensor("x2", [D, TL], F32, kind="ExternalOutput").ap()

    es = contextlib.ExitStack()
    with es:
        E = es.enter_context
        xn1b = E(nc.sbuf_tensor("xn1b", [128, NCH, TL], BF16))
        ct1 = E(nc.sbuf_tensor("ct1", [128, NCH, TL], BF16))
        a1tb = E(nc.sbuf_tensor("a1tb", [128, NCH, 32], BF16))
        a1b = E(nc.sbuf_tensor("a1b", [32, D], BF16))
        b1tb = E(nc.sbuf_tensor("b1tb", [32, D], BF16))
        w1nf = E(nc.sbuf_tensor("w1nf", [128, D], F32))
        fw1b = E(nc.sbuf_tensor("fw1b", [128, NCH, 128], BF16))
        magc = E(nc.sbuf_tensor("magc_s", [128, 32], F32))
        u1f = E(nc.sbuf_tensor("u1f", [32, TL], F32))
        u1b = E(nc.sbuf_tensor("u1b", [32, TL], BF16))
        effb = E(nc.sbuf_tensor("effb", [128, 512], BF16))
        effs = E(nc.sbuf_tensor("effs", [128, 512], BF16))
        rs1 = E(nc.sbuf_tensor("rs1", [128, 1], F32))
        cnacc = E(nc.sbuf_tensor("cnacc", [128, 32], F32))
        rsc = E(nc.sbuf_tensor("rsc", [128, 32], F32))
        resf = E(nc.sbuf_tensor("resf", [128, TL], F32))
        gsb = E(nc.sbuf_tensor("gsb", [128, TL], F32))
        cf = E(nc.sbuf_tensor("cf", [128, TL], F32))
        psA = E(nc.psum_tensor("psA", [128, 4, 512], F32))
        block = E(nc.Block())
        p = Prog(nc, es)

        xn1r = xn1_in.rearrange("(c p) t -> p c t", p=128)
        ctr = ctx_in.rearrange("(c p) t -> p c t", p=128)
        x1r = x1_in.rearrange("(c p) t -> p c t", p=128)
        x2r = x2_out.rearrange("(c p) t -> p c t", p=128)

        for i in range(4):
            sl = slice(i * 8, (i + 1) * 8)
            dma(p, "sync", xn1b[:, sl, :], xn1r[:, sl, :])
            dma(p, "sync", ct1[:, sl, :], ctr[:, sl, :])
        dma(p, "gpsimd", a1tb[:], a1t4_in)
        dma(p, "gpsimd", a1b[:], a1_in)
        dma(p, "gpsimd", b1tb[:], b1t_in)
        dma(p, "sync", magc[:], mag_in)
        p.op("vector", lambda e: e.memset(cnacc[:], 0.0))

        for half in range(2):
            mms = [lambda e, c=c, half=half, st=(c == 0), sp=(c == NCH - 1):
                   e.matmul(psA[0:32, half, :], a1tb[:, c, :], xn1b[:, c, half * 512:(half + 1) * 512], start=st, stop=sp)
                   for c in range(NCH)]
            p.chain("tensor", mms)
        p.op("scalar", lambda e: e.copy(u1f[:, 0:512], psA[0:32, 0, :]))
        p.op("scalar", lambda e: e.copy(u1f[:, 512:1024], psA[0:32, 1, :]))
        p.op("vector", lambda e: e.tensor_copy(u1b[:], u1f[:]))

        for oc in range(32):
            dma(p, "sync", w1nf[:], w1n4_in[oc].rearrange("p c i -> p (c i)"))
            for ib in range(8):
                isl = slice(ib * 512, (ib + 1) * 512)
                p.op("tensor", lambda e, oc=oc, isl=isl: e.matmul(
                    psA[:, 2, :], b1tb[:, oc * 128:(oc + 1) * 128], a1b[:, isl], start=True, stop=True))
                p.op("vector", lambda e, isl=isl: e.tensor_tensor(effb[:], psA[:, 2, :], w1nf[:, isl], op=ALU.add))
                p.op("scalar", lambda e: e.activation(effs[:], effb[:], AF.Square))
                p.op("vector", lambda e: e.reduce_sum(rs1[:], effs[:], axis=AX.X))
                p.op("vector", lambda e, oc=oc: e.tensor_tensor(cnacc[:, oc:oc + 1], cnacc[:, oc:oc + 1], rs1[:], op=ALU.add))
        p.op("scalar", lambda e: e.activation(cnacc[:], cnacc[:], AF.Sqrt))
        p.op("vector", lambda e: e.reciprocal(rsc[:], cnacc[:]))
        p.op("vector", lambda e: e.tensor_tensor(rsc[:], rsc[:], magc[:], op=ALU.mult))

        for oc in range(32):
            dma(p, "gpsimd", fw1b[:], w1t4_in[oc])
            for half in range(2):
                hs = slice(half * 512, (half + 1) * 512)
                mms = [lambda e, c=c, half=half, hs=hs:
                       e.matmul(psA[:, half, :], fw1b[:, c, :], xn1b[:, c, hs], start=(c == 0), stop=False)
                       for c in range(NCH)]
                mms.append(lambda e, half=half, hs=hs, oc=oc: e.matmul(
                    psA[:, half, :], b1tb[:, oc * 128:(oc + 1) * 128], u1b[:, hs], start=False, stop=True))
                p.chain("tensor", mms)
            p.op("vector", lambda e, oc=oc: e.tensor_scalar(resf[:, 0:512], psA[:, 0, :], rsc[:, oc:oc + 1], None, op0=ALU.mult))
            p.op("vector", lambda e, oc=oc: e.tensor_scalar(resf[:, 512:1024], psA[:, 1, :], rsc[:, oc:oc + 1], None, op0=ALU.mult))
            p.op("scalar", lambda e: e.activation(gsb[:], resf[:], AF.Silu))
            p.op("vector", lambda e, oc=oc: e.tensor_copy(cf[:], ct1[:, oc, :]))
            p.op("vector", lambda e: e.tensor_tensor(cf[:], cf[:], gsb[:], op=ALU.mult))
            dma(p, "sync", gsb[:], x1r[:, oc, :])
            p.op("vector", lambda e: e.tensor_tensor(cf[:], gsb[:], cf[:], op=ALU.add))
            dma(p, "sync", x2r[:, oc, :], cf[:])
        p.finish(block)
    return nc



def _run(nc, in_maps):
    res = run_bass_kernel_spmd(nc, in_maps, core_ids=list(range(NCORE)))
    return res.results


def kernel(L0_attn_x, L0_attn_Wq, L0_attn_Wkv, L0_attn_Wo, L0_attn_bo,
           L0_ffn_weight, L0_ffn_lora_A, L0_ffn_lora_B, L0_ffn_gw1, L0_ffn_gb1,
           L0_ffn_ln_gamma, L0_ffn_ln_beta, L0_ffn_gw2, L0_ffn_gb2,
           L1_attn_wq, L1_attn_bq, L1_attn_wk, L1_attn_bk, L1_attn_wv,
           L1_attn_bv, L1_attn_wl, L1_attn_bl, L1_ffn_W, L1_ffn_lora_A,
           L1_ffn_lora_B, L1_ffn_magnitude):
    c32 = np.ascontiguousarray
    xT = c32(np.asarray(L0_attn_x, dtype=np.float32).reshape(T, D).T)
    ident = _ident()

    gw1t = c32(np.asarray(L0_ffn_gw1, np.float32).T)
    nc_a = build_A()
    in_a = []
    for c in range(NCORE):
        in_a.append({
            "x": c32(xT[:, c * TL:(c + 1) * TL]),
            "gw1t": gw1t,
            "gb1c": c32(np.asarray(L0_ffn_gb1, np.float32).reshape(32, 1)),
            "gam": c32(np.asarray(L0_ffn_ln_gamma, np.float32).reshape(1, 32)),
            "bet": c32(np.asarray(L0_ffn_ln_beta, np.float32).reshape(1, 32)),
            "gw2t": c32(np.asarray(L0_ffn_gw2, np.float32).T),
            "gb2": c32(np.asarray(L0_ffn_gb2, np.float32).reshape(1, 8)),
            "ident": ident,
        })
    res_a = _run(nc_a, in_a)
    xn_full = np.concatenate([res_a[c]["xn"] for c in range(NCORE)], axis=1)
    cnts = np.stack([res_a[c]["cnt"].reshape(2, 8) for c in range(NCORE)])
    wgs = [res_a[c]["wg"] for c in range(NCORE)]

    xn4 = c32(xn_full.reshape(NCH, 128, 32, 256).transpose(2, 1, 0, 3))

    def tile_pcf(w, width):
        return c32(w.reshape(NCH, 128, width).transpose(1, 0, 2))

    nc_b = build_attn(l1=False)
    tri = np.where(np.arange(128)[:, None] > np.arange(128)[None, :],
                   np.float32(NEG), np.float32(0.0)).astype(np.float32)
    Wq = np.asarray(L0_attn_Wq, np.float32)
    Wkv = np.asarray(L0_attn_Wkv, np.float32)
    in_b = []
    for c in range(NCORE):
        wq_sl = c32(Wq[:, 512 * c:512 * (c + 1)])
        wkv_sl = c32(Wkv[:, 256 * c:256 * (c + 1)])
        in_b.append({
            "xn4": xn4,
            "wq4": tile_pcf(wq_sl, 512),
            "wkv4": tile_pcf(wkv_sl, 256),
            "tri": tri,
        })
    res_b = _run(nc_b, in_b)
    ctx_full = np.concatenate([res_b[c]["ctx"] for c in range(NCORE)], axis=0)

    nc_c = build_C()
    wo4 = _tile_w(np.asarray(L0_attn_Wo, np.float32))
    fw4 = _tile_w(c32(np.asarray(L0_ffn_weight, np.float32).T))
    la_cat = c32(np.asarray(L0_ffn_lora_A, np.float32).transpose(1, 0, 2).reshape(D, 128))
    la4 = c32(la_cat.reshape(NCH, 128, 128).transpose(1, 0, 2))
    lb_cat = c32(np.asarray(L0_ffn_lora_B, np.float32).reshape(128, D))
    boc = _to_col(np.asarray(L0_attn_bo, np.float32))
    sel2 = np.zeros((16, 2), np.float32)
    sel2[0::2, 0] = 1.0
    sel2[1::2, 1] = 1.0
    ssel = np.zeros((8, 128), np.float32)
    for e_ in range(8):
        ssel[e_, e_ * 16:(e_ + 1) * 16] = 1.0
    cnt16 = c32(cnts.reshape(16, 8))
    in_c = []
    for c in range(NCORE):
        in_c.append({
            "x": c32(xT[:, c * TL:(c + 1) * TL]),
            "xn": c32(xn_full[:, c * TL:(c + 1) * TL]),
            "ctxt": c32(ctx_full[:, c * TL:(c + 1) * TL]),
            "wo4": wo4,
            "fw4": fw4,
            "boc": boc,
            "la4": la4,
            "lb": lb_cat,
            "cnts": cnt16,
            "sel2": sel2,
            "wg": wgs[c],
            "ssel": ssel,
            "ident": ident,
        })
    res_c = _run(nc_c, in_c)
    x1_sh = [res_c[c]["x1"] for c in range(NCORE)]
    xn1_full = np.concatenate([res_c[c]["xn1"] for c in range(NCORE)], axis=1)
    xn14 = c32(xn1_full.reshape(NCH, 128, 32, 256).transpose(2, 1, 0, 3))

    nc_d = build_D()
    wq1 = np.asarray(L1_attn_wq, np.float32)
    wk1 = np.asarray(L1_attn_wk, np.float32)
    wv1 = np.asarray(L1_attn_wv, np.float32)
    wq4 = _tile_w(c32(wq1.T))
    wk4 = _tile_w(c32(wk1.T))
    wv4 = _tile_w(c32(wv1.T))
    bqc = _to_col(np.asarray(L1_attn_bq, np.float32))
    bkc = _to_col(np.asarray(L1_attn_bk, np.float32))
    bvc = _to_col(np.asarray(L1_attn_bv, np.float32))
    identb = np.eye(128).astype(ml_dtypes.bfloat16)
    in_d = []
    for c in range(NCORE):
        in_d.append({
            "xn1": c32(xn1_full[:, c * TL:(c + 1) * TL]),
            "wq4": wq4, "wk4": wk4, "wv4": wv4,
            "bqc": bqc, "bkc": bkc, "bvc": bvc,
            "identb": identb,
        })
    res_d = _run(nc_d, in_d)
    ctx1_sh = [np.asarray(res_d[c]["ctx"]) for c in range(NCORE)]

    nc_e = build_E()
    W1 = np.asarray(L1_ffn_W, np.float32)
    w1t4 = _tile_w(c32(W1.T))
    w1n4 = c32(W1.reshape(32, 128, NCH, 128))
    A1 = np.asarray(L1_ffn_lora_A, np.float32)
    B1 = np.asarray(L1_ffn_lora_B, np.float32)
    a1t4 = c32(A1.T.reshape(NCH, 128, 32).transpose(1, 0, 2))
    b1t = c32(B1.T)
    magc = _to_col(np.asarray(L1_ffn_magnitude, np.float32))
    in_e = []
    for c in range(NCORE):
        in_e.append({
            "x1": x1_sh[c],
            "xn1": c32(xn1_full[:, c * TL:(c + 1) * TL]),
            "ctxt": ctx1_sh[c],
            "w1t4": w1t4,
            "w1n4": w1n4,
            "a1t4": a1t4,
            "a1": A1,
            "b1t": b1t,
            "magc": magc,
        })
    res_e = _run(nc_e, in_e)
    x2 = np.concatenate([res_e[c]["x2"] for c in range(NCORE)], axis=1)
    return np.ascontiguousarray(x2.T).reshape(1, T, D).astype(np.float32)
